# version 01 (final)
# baseline (speedup 1.0000x reference)
import numpy as np
from contextlib import ExitStack
import concourse.bass as bass
import concourse.mybir as mybir
from concourse.bass_utils import run_bass_kernel_spmd

F32 = mybir.dt.float32
BF16 = mybir.dt.bfloat16
I32 = mybir.dt.int32
ALU = mybir.AluOpType
AF = mybir.ActivationFunctionType

D = 1024
T = 2048
Q = 512
NCH = T // Q
NF = 512
H1 = 4096
H2 = 3072
EPS = 1e-5
BUILD_UPTO = 4
L0F_UPTO = 99
TRUNC = False
DUMP_SET = set()

ENGS = ("sync", "tensor", "vector", "scalar", "gpsimd")


class Buf:
    __slots__ = ("name", "last_writer", "readers")
    def __init__(self, name):
        self.name = name; self.last_writer = None; self.readers = []


class OpRec:
    __slots__ = ("eng", "fn", "dma", "deps", "marked", "cum", "idx", "xdep")
    def __init__(self, eng, fn, dma):
        self.eng = eng; self.fn = fn; self.dma = dma
        self.deps = []; self.marked = False; self.cum = None; self.idx = -1
        self.xdep = False


class Prog:
    def __init__(self):
        self.ops = []
        self.last_on = {e: None for e in ENGS}

    def op(self, eng, fn, reads=(), writes=(), dma=False, deps=()):
        o = OpRec(eng, fn, dma)
        o.idx = len(self.ops)
        ds = set()
        for b in reads:
            if b.last_writer is not None:
                ds.add(b.last_writer)
        for b in writes:
            if b.last_writer is not None:
                ds.add(b.last_writer)
            for r in b.readers:
                ds.add(r)
        for d in deps:
            if d is not None:
                ds.add(d)
        o.deps = [d for d in ds if d is not o]
        for b in reads:
            b.readers.append(o)
        for b in writes:
            b.last_writer = o
            b.readers = []
        self.ops.append(o)
        self.last_on[eng] = o
        return o

    def barrier(self):
        prev = dict(self.last_on)
        for e in ENGS:
            deps = [p for pe, p in prev.items() if p is not None and pe != e]
            self.op(e, lambda eng: eng.nop(), deps=deps)

    def finalize(self, eng="gpsimd"):
        deps = [p for p in self.last_on.values() if p is not None]
        self.op(eng, lambda g: g.nop(), deps=deps)

    def emit(self, nc, ctx):
        semC = {e: ctx.enter_context(nc.semaphore("sc_" + e)) for e in ENGS}
        semD = {e: ctx.enter_context(nc.semaphore("sd_" + e)) for e in ENGS}
        for o in self.ops:
            if o.dma:
                o.marked = True
            for d in o.deps:
                d.marked = True
                if d.eng != o.eng:
                    d.xdep = True
        cum = {}
        for o in self.ops:
            if o.marked:
                key = (o.eng, o.dma)
                cum[key] = cum.get(key, 0) + (16 if o.dma else 1)
                o.cum = cum[key]
        waited = {e: {} for e in ENGS}
        per_eng = {e: [] for e in ENGS}
        for o in self.ops:
            need = {}
            for d in o.deps:
                if d.eng == o.eng and not d.dma:
                    continue
                key = (d.eng, d.dma)
                if need.get(key, 0) < d.cum:
                    need[key] = d.cum
            w = waited[o.eng]
            ws = []
            for key, v in sorted(need.items()):
                if w.get(key, 0) < v:
                    w[key] = v
                    ws.append((key, v))
            per_eng[o.eng].append((o, ws))
        with nc.Block() as block:
            def mk(e):
                def section(eng):
                    for o, ws in per_eng[e]:
                        for (pe, isdma), v in ws:
                            eng.wait_ge(semD[pe] if isdma else semC[pe], v)
                        inst = o.fn(eng)
                        if o.marked:
                            assert inst is not None, "marked op returned None"
                            if (not o.dma
                                    and e in ("vector", "gpsimd", "scalar")):
                                inst = eng.drain()
                            inst.then_inc(semD[e] if o.dma else semC[e],
                                          16 if o.dma else 1)
                return section
            block.sync(mk("sync"))
            block.tensor(mk("tensor"))
            block.vector(mk("vector"))
            block.scalar(mk("scalar"))
            block.gpsimd(mk("gpsimd"))


def host_tables():
    n = np.arange(D, dtype=np.float64)
    j = np.arange(NF, dtype=np.float64)
    ang = 2.0 * np.pi * np.outer(n, j) / 1024.0
    cosw = np.cos(ang)
    sinw = -np.sin(ang)
    sinw[:, 0] = np.cos(np.pi * n)
    g = np.full(NF, 2.0 / 1024.0); g[0] = 1.0 / 1024.0
    gcw = np.cos(ang) * g[None, :]
    gsw = -np.sin(ang) * g[None, :]
    gsw[:, 0] = np.cos(np.pi * n) * (1.0 / 1024.0)
    i = np.arange(1, 513, dtype=np.float64)
    tsin = -2.0 * np.sin(2.0 * np.pi * np.outer(i, j) / 1024.0)
    tsin[511, :] = 0.0
    thi = tsin.astype(np.float32)
    tlo = (tsin - thi.astype(np.float64)).astype(np.float32)
    return (cosw.astype(np.float32), sinw.astype(np.float32),
            gcw.astype(np.float32), gsw.astype(np.float32), thi, tlo)


def col8(v):
    return np.ascontiguousarray(np.asarray(v, np.float32).reshape(8, 128).T)


def stage_inputs(inputs):
    cosw, sinw, gcw, gsw, thi, tlo = host_tables()
    cw1 = np.asarray(inputs["L0_ffn_conv_w"], np.float32)[:, 0, :]
    cw4 = np.asarray(inputs["L1_attn_conv_w"], np.float32)[:, 0, :]
    sh = {
        "cosw": cosw, "sinw": sinw, "gcw": gcw, "gsw": gsw,
        "tsin_hi": thi, "tsin_lo": tlo,
        "ident": np.eye(128, dtype=np.float32),
        "apar": np.ascontiguousarray(
            np.asarray(inputs["L0_attn_a_params"], np.float32).reshape(512, 1)),
        "bw": np.ascontiguousarray(np.asarray(inputs["L0_attn_B_w"], np.float32)),
        "cwt": np.ascontiguousarray(np.asarray(inputs["L0_attn_C_w"], np.float32).T),
        "l0d": col8(inputs["L0_attn_D"]),
        "w1t": np.ascontiguousarray(np.asarray(inputs["L0_ffn_w1"], np.float32).T),
        "w2t": np.ascontiguousarray(np.asarray(inputs["L0_ffn_w2"], np.float32).T),
        "cv1": np.ascontiguousarray(np.concatenate(
            [cw1[:, k].reshape(32, 128).T for k in range(3)], axis=1)),
        "cb1": np.ascontiguousarray(
            np.asarray(inputs["L0_ffn_conv_b"], np.float32).reshape(32, 128).T),
        "win": np.ascontiguousarray(np.asarray(inputs["L1_attn_W_in"], np.float32)),
        "cv4": np.ascontiguousarray(np.concatenate(
            [cw4[:, k].reshape(8, 128).T for k in range(4)], axis=1)),
        "cb4": col8(inputs["L1_attn_conv_b"]),
        "wxp": np.ascontiguousarray(np.asarray(inputs["L1_attn_W_xproj"], np.float32)),
        "dpar": col8(inputs["L1_attn_D_param"]),
        "wout": np.ascontiguousarray(np.asarray(inputs["L1_attn_W_out"], np.float32)),
        "wgt": np.ascontiguousarray(np.asarray(inputs["L1_ffn_w_gate"], np.int32).T),
        "wut": np.ascontiguousarray(np.asarray(inputs["L1_ffn_w_up"], np.int32).T),
        "wdt": np.ascontiguousarray(np.asarray(inputs["L1_ffn_w_down"], np.int32).T),
    }
    u = np.asarray(inputs["L0_attn_u"], np.float32)
    logA = np.asarray(inputs["L1_attn_log_A"], np.float32)
    maps = []
    for core in range(8):
        b, h = core // 2, core % 2
        m = dict(sh)
        m["x0"] = np.ascontiguousarray(u[b, h * T:(h + 1) * T, :].T)
        la = logA[:, 8 * h:8 * h + 8]
        m["loga"] = np.ascontiguousarray(
            la.reshape(8, 128, 8).transpose(1, 0, 2).reshape(128, 64))
        m["role"] = np.array([[h]], np.int32)
        m["gatep"] = np.array([[1.0 if h == 1 else 0.0]], np.float32)
        m["gater"] = np.array([[1.0 if h == 0 else 0.0]], np.float32)
        maps.append(m)
    return maps


def build_program(nonce_len=1, magic=1):
    nc = bass.Bass("TRN2", target_bir_lowering=False)
    ctx = ExitStack()
    P = Prog()
    dr = {}

    for nm, shp, dt in [
        ("x0", (D, T), F32), ("cosw", (D, NF), F32), ("sinw", (D, NF), F32),
        ("gcw", (D, NF), F32), ("gsw", (D, NF), F32),
        ("tsin_hi", (512, 512), F32), ("tsin_lo", (512, 512), F32),
        ("ident", (128, 128), F32), ("apar", (512, 1), F32),
        ("bw", (D, D), F32), ("cwt", (D, D), F32), ("l0d", (128, 8), F32),
        ("w1t", (D, H1), F32), ("w2t", (H1, D), F32),
        ("cv1", (128, 96), F32), ("cb1", (128, 32), F32),
        ("win", (D, 2 * D), F32), ("cv4", (128, 32), F32), ("cb4", (128, 8), F32),
        ("wxp", (D, 33), F32), ("loga", (128, 64), F32), ("dpar", (128, 8), F32),
        ("wout", (D, D), F32),
        ("wgt", (D, H2), I32), ("wut", (D, H2), I32), ("wdt", (H2, D), I32),
        ("role", (1, 1), I32), ("gatep", (1, 1), F32), ("gater", (1, 1), F32),
    ]:
        dr[nm] = nc.dram_tensor(nm, list(shp), dt, kind="ExternalInput").ap()
    dr["nonce"] = nc.dram_tensor("nonce", [1, nonce_len], F32,
                                 kind="ExternalInput").ap()
    xout = nc.dram_tensor("xout", [D, T], F32, kind="ExternalOutput").ap()
    dbg = nc.dram_tensor("dbg", [128, 8192], F32, kind="ExternalOutput").ap()
    dbg_state = {"off": 0}

    for nm, shp in [("w1q", (D, H1)), ("w2q", (H1, D)), ("wgb", (D, H2)),
                    ("wub", (D, H2)), ("wdb", (H2, D)), ("winb", (D, 2 * D)),
                    ("zscr", (D, T))]:
        dr[nm] = nc.dram_tensor(nm, list(shp), BF16, kind="Internal").ap()

    mbB = {}
    def mbox(name, shape, dt):
        dr[name] = nc.dram_tensor(name, list(shape), dt, kind="Internal",
                                  addr_space="Shared").ap()
        dr[name + "_f"] = nc.dram_tensor(name + "_f", [1, 1], I32,
                                         kind="Internal", addr_space="Shared").ap()
        mbB[name] = Buf("mb_" + name)
        mbB[name + "_f"] = Buf("mb_" + name + "_f")
    mbox("m1a", (128, 8), F32); mbox("m1b", (128, 8), F32)
    mbox("m2a", (128, 32), BF16); mbox("m2b", (128, 32), BF16)
    mbox("m3a", (128, 32), BF16); mbox("m3b", (128, 32), BF16)
    mbox("m4a", (128, 24), BF16); mbox("m4b", (128, 24), BF16)
    mbox("m5a", (D, T), BF16); mbox("m5b", (D, T), BF16)
    mbox("m5ra", (33, T), BF16); mbox("m5rb", (33, T), BF16)
    mbox("m6a", (128, 16384), BF16); mbox("m6b", (128, 16384), BF16)

    class Tl:
        __slots__ = ("t", "b")
        def __init__(self, t, name):
            self.t = t; self.b = Buf(name)
        def __getitem__(self, sl):
            return self.t[sl]

    def sb(name, shape, dt, cc=None):
        return Tl((cc or ctx).enter_context(
            nc.sbuf_tensor(name, list(shape), dt)), name)

    x = sb("x_sb", (128, 8 * T), BF16)
    idf = sb("id_f", (128, 128), F32)
    onef = sb("ones_f", (128, 512), F32)
    oneh = sb("ones_h", (128, 512), BF16)
    rowp = sb("rowp", (1, 4 * Q), F32)
    gates = sb("gates", (1, 2), F32)
    gcol = sb("gcol", (128, 2), F32)
    rolesb = sb("rolesb", (1, 1), I32)
    flag1 = sb("flag1", (1, 1), I32)
    frd = sb("frd", (1, 1), I32)
    dcol = sb("dcol", (128, 8), F32)

    dbf = sb("dbf", (128, 512), F32)

    def DUMP(ap, tl, width, label, npart=128):
        if label not in DUMP_SET:
            return
        off = dbg_state["off"]
        assert off + width <= 8192, label
        CP("vector", dbf[0:npart, 0:width], ap, r=[tl], w=[dbf])
        P.op("gpsimd", lambda e, o=dbg[0:128, off:off + width],
             i=dbf[0:128, 0:width]: e.dma_start(out=o, in_=i),
             reads=[dbf.b], dma=True)
        dbg_state["off"] = off + width
        print("DBG", label, "at", off, "w", width)

    pb = [Tl(ctx.enter_context(nc.psum_tensor("pb%d" % k, [128, 512], F32)),
             "pb%d" % k) for k in range(8)]

    psem = ctx.enter_context(nc.semaphore("psem"))
    mbsem = ctx.enter_context(nc.semaphore("mbsem"))
    regs = {}
    mbst = {"sends": 0}

    def get_regs(g):
        if "r" not in regs:
            regs["r"] = (g.alloc_register("r_role"), g.alloc_register("r_cnt"),
                         g.alloc_register("r_f"), g.alloc_register("r_p"))
        return regs["r"]

    def send_box(pay_ap, boxa, boxb, pay_tiles):
        def fsend(g, pay=pay_ap, A=boxa, B=boxb):
            r_role = get_regs(g)[0]
            g.reg_load(r_role, rolesb[0:1, 0:1])
            with g.If_eq(r_role, 0):
                g.dma_start(out=dr[A][:], in_=pay).then_inc(mbsem, 16)
            with g.Else():
                g.dma_start(out=dr[B][:], in_=pay).then_inc(mbsem, 16)
            g.end_ifs()
            mbst["n"] = mbst["sends"] = mbst["sends"] + 1
            g.wait_ge(mbsem, mbst["sends"] * 16)
            with g.If_eq(r_role, 0):
                g.dma_start(out=dr[A + "_f"][:], in_=flag1[:]).then_inc(mbsem, 16)
            with g.Else():
                g.dma_start(out=dr[B + "_f"][:], in_=flag1[:]).then_inc(mbsem, 16)
            g.end_ifs()
            mbst["sends"] += 1
            g.wait_ge(mbsem, mbst["sends"] * 16)
            return g.nop()
        return P.op("gpsimd", fsend,
                    reads=[t.b for t in pay_tiles] + [rolesb.b, flag1.b],
                    writes=[mbB[boxa], mbB[boxb], mbB[boxa + "_f"], mbB[boxb + "_f"]])

    def poll_box(out_ap, boxa, boxb, out_tiles, isl=None):
        def fpoll(g, out=out_ap, A=boxa, B=boxb, s=isl):
            r_role, r_cnt, r_f, r_p = get_regs(g)
            g.reg_load(r_role, rolesb[0:1, 0:1])
            if not regs.get("cnt_init"):
                g.reg_mov(r_cnt, 0)
                regs["cnt_init"] = True
            def cond():
                g.reg_add(r_cnt, r_cnt, 16)
                with g.If_eq(r_role, 0):
                    g.dma_start(out=frd[:], in_=dr[B + "_f"][:]).then_inc(psem, 16)
                with g.Else():
                    g.dma_start(out=frd[:], in_=dr[A + "_f"][:]).then_inc(psem, 16)
                g.end_ifs()
                g.wait_ge(psem, r_cnt)
                g.reg_load(r_f, frd[0:1, 0:1])
                g.reg_alu(r_p, r_f, magic, ALU.not_equal)
                return r_p
            with g.While(cond):
                g.nop()
            with g.If_eq(r_role, 0):
                g.dma_start(out=out, in_=(dr[B][:] if s is None else dr[B][s])
                            ).then_inc(psem, 16)
            with g.Else():
                g.dma_start(out=out, in_=(dr[A][:] if s is None else dr[A][s])
                            ).then_inc(psem, 16)
            g.end_ifs()
            g.reg_add(r_cnt, r_cnt, 16)
            g.wait_ge(psem, r_cnt)
            return g.nop()
        return P.op("gpsimd", fpoll,
                    reads=[rolesb.b, frd.b, mbB[boxa], mbB[boxb],
                           mbB[boxa + "_f"], mbB[boxb + "_f"]],
                    writes=[t.b for t in out_tiles])

    def poll_flag(flagname):
        def fpf(g, nm=flagname):
            r_role, r_cnt, r_f, r_p = get_regs(g)
            if not regs.get("cnt_init"):
                g.reg_mov(r_cnt, 0)
                regs["cnt_init"] = True
            def cond():
                g.reg_add(r_cnt, r_cnt, 16)
                g.dma_start(out=frd[:], in_=dr[nm][:]).then_inc(psem, 16)
                g.wait_ge(psem, r_cnt)
                g.reg_load(r_f, frd[0:1, 0:1])
                g.reg_alu(r_p, r_f, magic, ALU.not_equal)
                return r_p
            with g.While(cond):
                g.nop()
            return g.nop()
        return P.op("gpsimd", fpf, reads=[frd.b, mbB[flagname]], writes=[frd.b])

    def send_multi(pays, boxa, boxb, pay_tiles):
        def fsend(g):
            r_role = get_regs(g)[0]
            g.reg_load(r_role, rolesb[0:1, 0:1])
            with g.If_eq(r_role, 0):
                for fo, ia in pays:
                    g.dma_start(out=fo(boxa), in_=ia).then_inc(mbsem, 16)
            with g.Else():
                for fo, ia in pays:
                    g.dma_start(out=fo(boxb), in_=ia).then_inc(mbsem, 16)
            g.end_ifs()
            mbst["sends"] += len(pays)
            g.wait_ge(mbsem, mbst["sends"] * 16)
            with g.If_eq(r_role, 0):
                g.dma_start(out=dr[boxa + "_f"][:], in_=flag1[:]).then_inc(mbsem, 16)
            with g.Else():
                g.dma_start(out=dr[boxb + "_f"][:], in_=flag1[:]).then_inc(mbsem, 16)
            g.end_ifs()
            mbst["sends"] += 1
            g.wait_ge(mbsem, mbst["sends"] * 16)
            return g.nop()
        return P.op("gpsimd", fsend,
                    reads=[t.b for t in pay_tiles] + [rolesb.b, flag1.b],
                    writes=[mbB[boxa], mbB[boxb], mbB[boxa + "_f"], mbB[boxb + "_f"]])

    def branched_dma(pairs0, pairs1, rbufs, wbufs, slow=False):
        n = len(pairs0)
        assert len(pairs1) == n
        def fb(g):
            r_role, r_cnt, r_f, r_p = get_regs(g)
            g.reg_load(r_role, rolesb[0:1, 0:1])
            if not regs.get("cnt_init"):
                g.reg_mov(r_cnt, 0)
                regs["cnt_init"] = True
            with g.If_eq(r_role, 0):
                for o, i in pairs0:
                    g.dma_start(out=o, in_=i, allow_slow_non_contiguous=slow
                                ).then_inc(psem, 16)
            with g.Else():
                for o, i in pairs1:
                    g.dma_start(out=o, in_=i, allow_slow_non_contiguous=slow
                                ).then_inc(psem, 16)
            g.end_ifs()
            g.reg_add(r_cnt, r_cnt, 16 * n)
            g.wait_ge(psem, r_cnt)
            return g.nop()
        return P.op("gpsimd", fb, reads=[rolesb.b] + list(rbufs),
                    writes=list(wbufs))

    def LD(dst_ap, src_ap, wtl, q="sync", rtls=(), slow=False):
        return P.op(q, lambda e, o=dst_ap, i=src_ap, s=slow:
                    e.dma_start(out=o, in_=i, allow_slow_non_contiguous=s),
                    writes=[wtl.b], reads=[getattr(t, "b", t) for t in rtls],
                    dma=True)

    def STO(dst_ap, src_ap, rtl, q="gpsimd", wbufs=(), rextra=()):
        return P.op(q, lambda e, o=dst_ap, i=src_ap: e.dma_start(out=o, in_=i),
                    reads=[rtl.b] + [t.b for t in rextra],
                    writes=list(wbufs), dma=True)

    def TT(eng, out_ap, a_ap, b_ap, op, r, w):
        return P.op(eng, lambda e, o=out_ap, a=a_ap, b=b_ap, p=op:
                    e.tensor_tensor(o, a, b, p),
                    reads=[t.b for t in r], writes=[t.b for t in w])

    def TS(eng, out_ap, in_ap, s1, s2, op0, r, w, op1=None):
        if s2 is None and op1 is None:
            return P.op(eng, lambda e, o=out_ap, i=in_ap, a=s1, p0=op0:
                        e.tensor_scalar(o, i, a, None, p0),
                        reads=[t.b for t in r], writes=[t.b for t in w])
        return P.op(eng, lambda e, o=out_ap, i=in_ap, a=s1, b=s2, p0=op0,
                    p1=(op1 or ALU.bypass): e.tensor_scalar(o, i, a, b, p0, p1),
                    reads=[t.b for t in r], writes=[t.b for t in w])

    def STT(eng, out_ap, in0_ap, s, in1_ap, op0, op1, r, w):
        return P.op(eng, lambda e, o=out_ap, i=in0_ap, c=s, j=in1_ap, p0=op0, p1=op1:
                    e.scalar_tensor_tensor(o, i, c, j, p0, p1),
                    reads=[t.b for t in r], writes=[t.b for t in w])

    def CP(eng, out_ap, in_ap, r, w):
        if eng == "scalar":
            return P.op(eng, lambda e, o=out_ap, i=in_ap: e.copy(o, i),
                        reads=[t.b for t in r], writes=[t.b for t in w])
        return P.op(eng, lambda e, o=out_ap, i=in_ap: e.tensor_copy(o, i),
                    reads=[t.b for t in r], writes=[t.b for t in w])

    def ACT(out_ap, in_ap, func, r, w, bias=0.0, scale=1.0):
        return P.op("scalar", lambda e, o=out_ap, i=in_ap, f=func, bi=bias, sc=scale:
                    e.activation(o, i, f, bias=bi, scale=sc),
                    reads=[t.b for t in r], writes=[t.b for t in w])

    def MM(out_ap, lhsT_ap, rhs_ap, start, stop, r, w):
        return P.op("tensor", lambda e, o=out_ap, l=lhsT_ap, x=rhs_ap, s0=start,
                    s1=stop: e.matmul(o, l, x, start=s0, stop=s1),
                    reads=[t.b for t in r], writes=[t.b for t in w])

    def TR(out_ap, in_ap, r, w):
        return P.op("tensor", lambda e, o=out_ap, i=in_ap:
                    e.transpose(o, i, idf[:, :]),
                    reads=[t.b for t in r] + [idf.b], writes=[t.b for t in w])

    def SCAN(out_ap, a_ap, b_ap, init_ap, r, w):
        return P.op("vector", lambda e, o=out_ap, a=a_ap, b=b_ap, i=init_ap:
                    e.tensor_tensor_scan(o, a, b, i, ALU.mult, ALU.add),
                    reads=[t.b for t in r], writes=[t.b for t in w])

    def RED(out_ap, in_ap, op, r, w, absval=False):
        return P.op("vector", lambda e, o=out_ap, i=in_ap, p=op, av=absval:
                    e.tensor_reduce(o, i, mybir.AxisListType.X, p,
                                    apply_absolute_value=av),
                    reads=[t.b for t in r], writes=[t.b for t in w])

    def MEMSET(eng, ap, val, w):
        return P.op(eng, lambda e, o=ap, v=val: e.memset(o, v),
                    writes=[t.b for t in w])



    def row(i, width=Q):
        return rowp[0:1, i * Q:i * Q + width]

    def ln_stats(srcf, src_tl, sq, width=Q):
        for k in range(8):
            ACT(sq[0:128, 0:width], srcf(k), AF.Square, r=[src_tl], w=[sq])
            MM(pb[7][0:1, 0:width], oneh[:, 0:1], sq[0:128, 0:width],
               start=(k == 0), stop=(k == 7), r=[sq, oneh], w=[pb[7]])
        for k in range(8):
            MM(pb[6][0:1, 0:width], oneh[:, 0:1], srcf(k),
               start=(k == 0), stop=(k == 7), r=[src_tl, oneh], w=[pb[6]])
        TS("vector", row(0, width), pb[6][0:1, 0:width], 1.0 / 1024.0, None,
           ALU.mult, r=[pb[6]], w=[rowp])
        TT("vector", row(3, width), row(0, width), row(0, width), ALU.mult,
           r=[rowp], w=[rowp])
        STT("vector", row(1, width), pb[7][0:1, 0:width], 1.0 / 1024.0,
            row(3, width), ALU.mult, ALU.subtract, r=[pb[7], rowp], w=[rowp])
        TS("vector", row(1, width), row(1, width), EPS, None, ALU.add,
           r=[rowp], w=[rowp])
        ACT(row(2, width), row(1, width), AF.Ln, r=[rowp], w=[rowp])
        ACT(row(2, width), row(2, width), AF.Exp, r=[rowp], w=[rowp], scale=-0.5)
        STT("vector", row(3, width), row(0, width), -1.0, row(2, width),
            ALU.mult, ALU.mult, r=[rowp], w=[rowp])
        MM(pb[6][:, 0:width], onef[0:1, 0:128], row(2, width), True, True,
           r=[rowp, onef], w=[pb[6]])
        MM(pb[7][:, 0:width], onef[0:1, 0:128], row(3, width), True, True,
           r=[rowp, onef], w=[pb[7]])

    def ln_apply(srcf, outf, src_tl, out_tl, width=Q):
        for k in range(8):
            TT("vector", outf(k), srcf(k), pb[6][:, 0:width], ALU.mult,
               r=[src_tl, pb[6]], w=[out_tl])
        for k in range(8):
            TT("vector", outf(k), outf(k), pb[7][:, 0:width], ALU.add,
               r=[out_tl, pb[7]], w=[out_tl])

    def post_ln_residual(ybf, sq, tmp, xoff_cols, width=Q):
        ln_stats(lambda k, w_=width: ybf[0:128, k * w_:(k + 1) * w_], ybf, sq, width)
        for k in range(8):
            TT("vector", tmp[0:128, 0:width],
               ybf[0:128, k * width:(k + 1) * width], pb[6][:, 0:width],
               ALU.mult, r=[ybf, pb[6]], w=[tmp])
            TT("vector", tmp[0:128, 0:width], tmp[0:128, 0:width],
               pb[7][:, 0:width], ALU.add, r=[tmp, pb[7]], w=[tmp])
            TT("vector", x[0:128, k * T + xoff_cols:k * T + xoff_cols + width],
               x[0:128, k * T + xoff_cols:k * T + xoff_cols + width],
               tmp[0:128, 0:width], ALU.add, r=[tmp, x], w=[x])

    def post_ln_residual2(ybf_, sq_, tmp_, xoff_cols, width):
        ln_stats(lambda k, w_=width: ybf_[0:128, k * 512:k * 512 + w_],
                 ybf_, sq_, width)
        for k in range(8):
            TT("vector", tmp_[0:128, 0:width],
               ybf_[0:128, k * 512:k * 512 + width], pb[6][:, 0:width],
               ALU.mult, r=[ybf_, pb[6]], w=[tmp_])
            TT("vector", tmp_[0:128, 0:width], tmp_[0:128, 0:width],
               pb[7][:, 0:width], ALU.add, r=[tmp_, pb[7]], w=[tmp_])
            TT("vector", x[0:128, k * T + xoff_cols:k * T + xoff_cols + width],
               x[0:128, k * T + xoff_cols:k * T + xoff_cols + width],
               tmp_[0:128, 0:width], ALU.add, r=[tmp_, x], w=[x])

    LD(idf[:, :], dr["ident"][:], idf)
    MEMSET("vector", onef[:, :], 1.0, w=[onef])
    MEMSET("vector", oneh[:, :], 1.0, w=[oneh])
    MEMSET("vector", flag1[:, :], magic, w=[flag1])
    LD(rolesb[:, :], dr["role"][:], rolesb, q="gpsimd")
    LD(gates[0:1, 0:1], dr["gatep"][:], gates, q="gpsimd")
    LD(gates[0:1, 1:2], dr["gater"][:], gates, q="gpsimd")
    LD(dcol[:, :], dr["l0d"][:], dcol, q="gpsimd")
    for i in range(2):
        MM(pb[4 + i][:, 0:1], onef[0:1, 0:128], gates[0:1, i:i + 1], True, True,
           r=[gates, onef], w=[pb[4 + i]])
        CP("vector", gcol[:, i:i + 1], pb[4 + i][:, 0:1],
           r=[pb[4 + i]], w=[gcol])

    l0 = ExitStack()
    l0p = ExitStack()
    bhre = sb("bhre", (128, 8 * 512), BF16, l0)
    bhim = sb("bhim", (128, 8 * 512), BF16, l0)
    cgre = sb("cgre", (128, 4 * 1024), BF16, l0)
    cgim = sb("cgim", (128, 4 * 1024), BF16, l0)
    lam = sb("lam", (128, 32), F32, l0)
    tpre = sb("tpre", (128, 4 * 512), F32, l0)
    tpim = sb("tpim", (128, 4 * 512), F32, l0)
    tmim = sb("tmim", (128, 4 * 512), F32, l0)
    pwr = sb("pwr", (128, 16), F32, l0)
    car = sb("car", (128, 16), F32, l0)
    prefin = sb("prefin", (128, 8), F32, l0)
    bnd2 = sb("bnd2", (128, 32), F32, l0)
    tab = sb("tab", (128, 2048), F32, l0p)
    bw_s = sb("bw_s", (128, 8 * 1024), BF16, l0p)
    cos_s = sb("cos_s", (128, 8 * 512), BF16, l0p)
    sin_s = sb("sin_s", (128, 8 * 512), BF16, l0p)
    tshi = sb("tshi", (128, 4 * 512), F32, l0p)
    tslo = sb("tslo", (128, 4 * 512), F32, l0p)
    apc = sb("apc", (128, 4), F32, l0p)

    for k in range(8):
        LD(tab[:, 0:T], dr["x0"][k * 128:(k + 1) * 128, :], tab)
        CP("scalar", x[0:128, k * T:(k + 1) * T], tab[:, 0:T], r=[tab], w=[x])

    def load_cast(dst, src_dr, rows, cols, q="sync"):
        nt = rows // 128
        for k in range(nt):
            LD(tab[:, (k % 2) * 1024:(k % 2) * 1024 + cols],
               src_dr[k * 128:(k + 1) * 128, 0:cols], tab, q=q)
            CP("scalar", dst[0:128, k * cols:(k + 1) * cols],
               tab[:, (k % 2) * 1024:(k % 2) * 1024 + cols], r=[tab], w=[dst])

    load_cast(cos_s, dr["cosw"], D, 512)
    load_cast(sin_s, dr["sinw"], D, 512)
    load_cast(bw_s, dr["bw"], D, 1024)
    for k in range(4):
        LD(tshi[:, k * 512:(k + 1) * 512], dr["tsin_hi"][k * 128:(k + 1) * 128, :],
           tshi)
        LD(tslo[:, k * 512:(k + 1) * 512], dr["tsin_lo"][k * 128:(k + 1) * 128, :],
           tslo)
        LD(apc[:, k:k + 1], dr["apar"][k * 128:(k + 1) * 128, 0:1], apc)

    for nm, tbl, dst in (("re", cos_s, bhre), ("im", sin_s, bhim)):
        for md in range(8):
            bank = pb[md % 4]
            for kd in range(8):
                MM(bank[:, 0:512],
                   bw_s[0:128, kd * 1024 + md * 128:kd * 1024 + (md + 1) * 128],
                   tbl[0:128, kd * 512:(kd + 1) * 512],
                   start=(kd == 0), stop=(kd == 7), r=[bw_s, tbl], w=[bank])
            CP("scalar", dst[0:128, md * 512:(md + 1) * 512], bank[:, 0:512],
               r=[bank], w=[dst])

    for jm in range(4):
        for k in range(4):
            MM(pb[4 + jm][:, 0:1],
               tshi[0:128, k * 512 + jm * 128:k * 512 + (jm + 1) * 128],
               apc[:, k:k + 1], start=(k == 0), stop=False,
               r=[tshi, apc], w=[pb[4 + jm]])
        for k in range(4):
            MM(pb[4 + jm][:, 0:1],
               tslo[0:128, k * 512 + jm * 128:k * 512 + (jm + 1) * 128],
               apc[:, k:k + 1], start=False, stop=(k == 3),
               r=[tslo, apc], w=[pb[4 + jm]])
        CP("vector", lam[:, jm:jm + 1], pb[4 + jm][:, 0:1],
           r=[pb[4 + jm]], w=[lam])

    OM, LRE, LIM, T1, T2, T3 = (lam[:, 0:4], lam[:, 4:8], lam[:, 8:12],
                                lam[:, 20:24], lam[:, 24:28], lam[:, 28:32])
    TT("vector", T1, OM, OM, ALU.mult, r=[lam], w=[lam])
    TS("vector", T3, T1, 1.0, None, ALU.add, r=[lam], w=[lam])
    DUMP(lam[:, 28:32], lam, 4, "T3_pre")
    ACT(pwr[:, 12:16], T3, AF.Ln, r=[lam], w=[pwr])
    DUMP(pwr[:, 12:16], pwr, 4, "ln_out")
    ACT(T2, pwr[:, 12:16], AF.Exp, r=[pwr], w=[lam], scale=-1.0)
    DUMP(lam[:, 24:28], lam, 4, "T2_post")
    if TRUNC:
        DUMP(lam[:, 0:32], lam, 32, "lam_trunc")
        P.finalize()
        P.emit(nc, ctx)
        l0p.close()
        l0.close()
        ctx.close()
        return nc
    TS("vector", T3, T1, -1.0, 1.0, ALU.mult, r=[lam], w=[lam], op1=ALU.add)
    TT("vector", LRE, T3, T2, ALU.mult, r=[lam], w=[lam])
    TS("vector", T3, OM, -2.0, None, ALU.mult, r=[lam], w=[lam])
    TT("vector", LIM, T3, T2, ALU.mult, r=[lam], w=[lam])

    DUMP(lam[:, 0:32], lam, 32, "lam")
    MEMSET("vector", tpre[:, :], 0.0, w=[tpre])
    MEMSET("vector", tpim[:, :], 0.0, w=[tpim])
    for jt in range(4):
        MEMSET("vector", tpre[0:128, jt * 512:jt * 512 + 1], 1.0, w=[tpre])
        CP("vector", tpre[0:128, jt * 512 + 1:jt * 512 + 2],
           lam[:, 4 + jt:5 + jt], r=[lam], w=[tpre])
        CP("vector", tpim[0:128, jt * 512 + 1:jt * 512 + 2],
           lam[:, 8 + jt:9 + jt], r=[lam], w=[tpim])
    CP("vector", pwr[:, 0:4], LRE, r=[lam], w=[pwr])
    CP("vector", pwr[:, 4:8], LIM, r=[lam], w=[pwr])
    s_ = 1
    while s_ < 512:
        TS("vector", pwr[:, 8:12], pwr[:, 4:8], -1.0, None, ALU.mult,
           r=[pwr], w=[pwr])
        for jt in range(4):
            a0 = jt * 512
            A = tpre[0:128, a0:a0 + s_]
            B = tpim[0:128, a0:a0 + s_]
            Ar = tpre[0:128, a0 + s_:a0 + 2 * s_]
            Ai = tpim[0:128, a0 + s_:a0 + 2 * s_]
            TS("vector", Ar, A, pwr[:, jt:jt + 1], None, ALU.mult,
               r=[tpre, pwr], w=[tpre])
            STT("vector", Ar, B, pwr[:, 8 + jt:9 + jt], Ar, ALU.mult, ALU.add,
                r=[tpim, pwr, tpre], w=[tpre])
            TS("vector", Ai, A, pwr[:, 4 + jt:5 + jt], None, ALU.mult,
               r=[tpre, pwr], w=[tpim])
            STT("vector", Ai, B, pwr[:, jt:jt + 1], Ai, ALU.mult, ALU.add,
                r=[tpim, pwr], w=[tpim])
        TT("vector", pwr[:, 12:16], pwr[:, 0:4], pwr[:, 4:8], ALU.mult,
           r=[pwr], w=[pwr])
        TT("vector", pwr[:, 8:12], pwr[:, 4:8], pwr[:, 4:8], ALU.mult,
           r=[pwr], w=[pwr])
        TT("vector", pwr[:, 0:4], pwr[:, 0:4], pwr[:, 0:4], ALU.mult,
           r=[pwr], w=[pwr])
        TT("vector", pwr[:, 0:4], pwr[:, 0:4], pwr[:, 8:12], ALU.subtract,
           r=[pwr], w=[pwr])
        TS("vector", pwr[:, 4:8], pwr[:, 12:16], 2.0, None, ALU.mult,
           r=[pwr], w=[pwr])
        s_ *= 2
    for jt in range(4):
        MEMSET("vector", bnd2[0:128, 0 * 4 + jt:0 * 4 + jt + 1], 1.0, w=[bnd2])
        MEMSET("vector", bnd2[0:128, 16 + jt:16 + jt + 1], 0.0, w=[bnd2])
        CP("vector", bnd2[0:128, 4 + jt:5 + jt], pwr[:, jt:jt + 1],
           r=[pwr], w=[bnd2])
        CP("vector", bnd2[0:128, 16 + 4 + jt:16 + 5 + jt], pwr[:, 4 + jt:5 + jt],
           r=[pwr], w=[bnd2])
    def cmul_cols(orei, oimi, are, aim, bre, bim, tmpc):
        TT("vector", lam[:, tmpc:tmpc + 1], are, bim, ALU.mult, r=[lam, bnd2, pwr],
           w=[lam])
        TT("vector", lam[:, tmpc + 1:tmpc + 2], are, bre, ALU.mult,
           r=[lam, bnd2, pwr], w=[lam])
        TT("vector", lam[:, tmpc + 2:tmpc + 3], aim, bim, ALU.mult,
           r=[lam, bnd2, pwr], w=[lam])
        TT("vector", orei, lam[:, tmpc + 1:tmpc + 2], lam[:, tmpc + 2:tmpc + 3],
           ALU.subtract, r=[lam], w=[bnd2])
        TT("vector", lam[:, tmpc + 2:tmpc + 3], aim, bre, ALU.mult,
           r=[lam, bnd2, pwr], w=[lam])
        TT("vector", oimi, lam[:, tmpc:tmpc + 1], lam[:, tmpc + 2:tmpc + 3],
           ALU.add, r=[lam], w=[bnd2])
    for jt in range(4):
        l5r, l5i = bnd2[0:128, 4 + jt:5 + jt], bnd2[0:128, 20 + jt:21 + jt]
        cmul_cols(bnd2[0:128, 8 + jt:9 + jt], bnd2[0:128, 24 + jt:25 + jt],
                  l5r, l5i, l5r, l5i, 20)
        cmul_cols(bnd2[0:128, 12 + jt:13 + jt], bnd2[0:128, 28 + jt:29 + jt],
                  bnd2[0:128, 8 + jt:9 + jt], bnd2[0:128, 24 + jt:25 + jt],
                  l5r, l5i, 20)
        cmul_cols(lam[:, 12 + jt:13 + jt], lam[:, 16 + jt:17 + jt],
                  bnd2[0:128, 8 + jt:9 + jt], bnd2[0:128, 24 + jt:25 + jt],
                  bnd2[0:128, 8 + jt:9 + jt], bnd2[0:128, 24 + jt:25 + jt],
                  20)
    TS("vector", tmim[:, :], tpim[:, :], -1.0, None, ALU.mult, r=[tpim], w=[tmim])
    DUMP(tpre[:, 0:512], tpre, 512, "tpre_jt0")
    DUMP(tpim[:, 0:512], tpim, 512, "tpim_jt0")
    DUMP(bnd2[:, 0:32], bnd2, 32, "bnd2")

    cwt_s = bw_s
    gc_s = cos_s
    gs_s = sin_s
    load_cast(gc_s, dr["gcw"], D, 512)
    load_cast(gs_s, dr["gsw"], D, 512)
    load_cast(cwt_s, dr["cwt"], D, 1024)
    for nm, tbl, dst in (("re", gc_s, cgre), ("im", gs_s, cgim)):
        for jm in range(4):
            for df in range(2):
                bank = pb[jm % 4]
                for kd in range(8):
                    MM(bank[:, 0:512],
                       tbl[0:128, kd * 512 + jm * 128:kd * 512 + (jm + 1) * 128],
                       cwt_s[0:128, kd * 1024 + df * 512:kd * 1024 + (df + 1) * 512],
                       start=(kd == 0), stop=(kd == 7), r=[tbl, cwt_s], w=[bank])
                CP("scalar", dst[0:128, jm * 1024 + df * 512:jm * 1024 + (df + 1) * 512],
                   bank[:, 0:512], r=[bank], w=[dst])

    P.barrier()
    l0p.close()
    lnst = sb("lnst", (128, 8 * T), BF16, l0)
    cre = sb("cre", (128, 4 * T), BF16, l0)
    cim = sb("cim", (128, 4 * T), BF16, l0)
    sq = sb("sq", (128, Q), BF16, l0)
    lmre = sb("lmre", (128, Q), F32, l0)
    lmim = sb("lmim", (128, Q), F32, l0)
    wre = sb("wre", (128, Q), F32, l0)
    wim = sb("wim", (128, Q), F32, l0)
    hre = sb("hre", (128, 4 * Q), BF16, l0)
    him = sb("him", (128, 4 * Q), BF16, l0)
    ybf = sb("ybf", (128, 8 * Q), BF16, l0)
    tmp = sb("tmp", (128, Q), BF16, l0)

    MEMSET("vector", car[:, 0:8], 0.0, w=[car])
    for c in range(NCH):
        xoff = c * Q
        ln_stats(lambda k, xo=xoff: x[0:128, k * T + xo:k * T + xo + Q], x, sq)
        ln_apply(lambda k, xo=xoff: x[0:128, k * T + xo:k * T + xo + Q],
                 lambda k, xo=xoff: lnst[0:128, k * T + xo:k * T + xo + Q],
                 x, lnst)
        for jt in range(4):
            for im_, (src, dstb) in enumerate(((bhre, 0), (bhim, 4))):
                bank = pb[jt + dstb] if False else pb[im_ * 4 + jt]
                for kd in range(8):
                    MM(bank[:, 0:Q],
                       src[0:128, kd * 512 + jt * 128:kd * 512 + (jt + 1) * 128],
                       lnst[0:128, kd * T + xoff:kd * T + xoff + Q],
                       start=(kd == 0), stop=(kd == 7),
                       r=[src, lnst], w=[bank])
        for jt in range(4):
            a0 = 0
            bre = bnd2[0:128, c * 4 + jt:c * 4 + jt + 1]
            bim = bnd2[0:128, 16 + c * 4 + jt:16 + c * 4 + jt + 1]
            TS("vector", pwr[:, 12:13], bim, -1.0, None, ALU.mult,
               r=[bnd2], w=[pwr])
            TS("vector", lmre[0:128, a0:a0 + Q], tpre[0:128, jt * 512:jt * 512 + Q],
               bre, None, ALU.mult, r=[tpre, bnd2], w=[lmre])
            STT("vector", lmre[0:128, a0:a0 + Q], tmim[0:128, jt * 512:jt * 512 + Q],
                bim, lmre[0:128, a0:a0 + Q], ALU.mult, ALU.add,
                r=[tmim, bnd2, lmre], w=[lmre])
            TS("vector", lmim[0:128, a0:a0 + Q], tpre[0:128, jt * 512:jt * 512 + Q],
               pwr[:, 12:13], None, ALU.mult, r=[tpre, pwr], w=[lmim])
            STT("vector", lmim[0:128, a0:a0 + Q], tmim[0:128, jt * 512:jt * 512 + Q],
                bre, lmim[0:128, a0:a0 + Q], ALU.mult, ALU.add,
                r=[tmim, bnd2, lmim], w=[lmim])
            TT("vector", wre[0:128, 0:Q], pb[jt][:, 0:Q],
               lmre[0:128, 0:Q], ALU.mult, r=[pb[jt], lmre], w=[wre])
            TT("vector", tmp[0:128, 0:Q], pb[4 + jt][:, 0:Q],
               lmim[0:128, 0:Q], ALU.mult, r=[pb[4 + jt], lmim], w=[tmp])
            TT("vector", wre[0:128, 0:Q], wre[0:128, 0:Q],
               tmp[0:128, 0:Q], ALU.subtract, r=[wre, tmp], w=[wre])
            TT("vector", wim[0:128, 0:Q], pb[4 + jt][:, 0:Q],
               lmre[0:128, 0:Q], ALU.mult, r=[pb[4 + jt], lmre], w=[wim])
            TT("vector", tmp[0:128, 0:Q], pb[jt][:, 0:Q],
               lmim[0:128, 0:Q], ALU.mult, r=[pb[jt], lmim], w=[tmp])
            TT("vector", wim[0:128, 0:Q], wim[0:128, 0:Q],
               tmp[0:128, 0:Q], ALU.add, r=[wim, tmp], w=[wim])
            SCAN(cre[0:128, jt * T + xoff:jt * T + xoff + Q], onef[:, 0:Q],
                 wre[0:128, 0:Q], car[:, jt:jt + 1],
                 r=[onef, wre, car], w=[cre])
            RED(pwr[:, 12:13], wre[0:128, 0:Q], ALU.add, r=[wre], w=[pwr])
            TT("vector", car[:, jt:jt + 1], car[:, jt:jt + 1], pwr[:, 12:13],
               ALU.add, r=[pwr, car], w=[car])
            SCAN(cim[0:128, jt * T + xoff:jt * T + xoff + Q], onef[:, 0:Q],
                 wim[0:128, 0:Q], car[:, 4 + jt:5 + jt],
                 r=[onef, wim, car], w=[cim])
            RED(pwr[:, 12:13], wim[0:128, 0:Q], ALU.add, r=[wim], w=[pwr])
            TT("vector", car[:, 4 + jt:5 + jt], car[:, 4 + jt:5 + jt],
               pwr[:, 12:13], ALU.add, r=[pwr, car], w=[car])

    DUMP(lnst[0:128, 0:512], lnst, 512, "lnst_t0c0")
    DUMP(wre[0:128, 0:512], wre, 512, "wre_jt0_c3")
    DUMP(cre[0:128, 0:512], cre, 512, "cre_jt0_c0")
    DUMP(car[:, 0:16], car, 16, "car")
    send_box(car[:, 0:8], "m1a", "m1b", [car])
    poll_box(prefin[:, 0:8], "m1a", "m1b", [prefin])
    for jt in range(4):
        Pre, Pim = prefin[:, jt:jt + 1], prefin[:, 4 + jt:5 + jt]
        Lre, Lim = lam[:, 12 + jt:13 + jt], lam[:, 16 + jt:17 + jt]
        TT("vector", lam[:, 20:21], Lre, Pre, ALU.mult, r=[lam, prefin], w=[lam])
        TT("vector", lam[:, 21:22], Lim, Pim, ALU.mult, r=[lam, prefin], w=[lam])
        TT("vector", lam[:, 22:23], lam[:, 20:21], lam[:, 21:22], ALU.subtract,
           r=[lam], w=[lam])
        TT("vector", car[:, 8 + jt:9 + jt], lam[:, 22:23], gcol[:, 0:1], ALU.mult,
           r=[lam, gcol], w=[car])
        TT("vector", lam[:, 20:21], Lre, Pim, ALU.mult, r=[lam, prefin], w=[lam])
        TT("vector", lam[:, 21:22], Lim, Pre, ALU.mult, r=[lam, prefin], w=[lam])
        TT("vector", lam[:, 22:23], lam[:, 20:21], lam[:, 21:22], ALU.add,
           r=[lam], w=[lam])
        TT("vector", car[:, 12 + jt:13 + jt], lam[:, 22:23], gcol[:, 0:1], ALU.mult,
           r=[lam, gcol], w=[car])
    for jt in range(4):
        TS("vector", cre[0:128, jt * T:(jt + 1) * T], cre[0:128, jt * T:(jt + 1) * T],
           car[:, 8 + jt:9 + jt], None, ALU.add, r=[cre, car], w=[cre])
        TS("vector", cim[0:128, jt * T:(jt + 1) * T], cim[0:128, jt * T:(jt + 1) * T],
           car[:, 12 + jt:13 + jt], None, ALU.add, r=[cim, car], w=[cim])

    for c in range(NCH):
        xoff = c * Q
        for jt in range(4):
            a0 = jt * Q
            bre = bnd2[0:128, c * 4 + jt:c * 4 + jt + 1]
            bim = bnd2[0:128, 16 + c * 4 + jt:16 + c * 4 + jt + 1]
            TS("vector", pwr[:, 12:13], bim, -1.0, None, ALU.mult,
               r=[bnd2], w=[pwr])
            TS("vector", lmre[0:128, 0:Q], tpre[0:128, jt * 512:jt * 512 + Q],
               bre, None, ALU.mult, r=[tpre, bnd2], w=[lmre])
            STT("vector", lmre[0:128, 0:Q], tpim[0:128, jt * 512:jt * 512 + Q],
                pwr[:, 12:13], lmre[0:128, 0:Q], ALU.mult, ALU.add,
                r=[tpim, pwr, lmre], w=[lmre])
            TS("vector", lmim[0:128, 0:Q], tpre[0:128, jt * 512:jt * 512 + Q],
               bim, None, ALU.mult, r=[tpre, bnd2], w=[lmim])
            STT("vector", lmim[0:128, 0:Q], tpim[0:128, jt * 512:jt * 512 + Q],
                bre, lmim[0:128, 0:Q], ALU.mult, ALU.add,
                r=[tpim, bnd2, lmim], w=[lmim])
            TT("vector", hre[0:128, a0:a0 + Q],
               cre[0:128, jt * T + xoff:jt * T + xoff + Q],
               lmre[0:128, 0:Q], ALU.mult, r=[cre, lmre], w=[hre])
            TT("vector", tmp[0:128, 0:Q],
               cim[0:128, jt * T + xoff:jt * T + xoff + Q],
               lmim[0:128, 0:Q], ALU.mult, r=[cim, lmim], w=[tmp])
            TT("vector", hre[0:128, a0:a0 + Q], hre[0:128, a0:a0 + Q],
               tmp[0:128, 0:Q], ALU.subtract, r=[hre, tmp], w=[hre])
            TT("vector", him[0:128, a0:a0 + Q],
               cim[0:128, jt * T + xoff:jt * T + xoff + Q],
               lmre[0:128, 0:Q], ALU.mult, r=[cim, lmre], w=[him])
            TT("vector", tmp[0:128, 0:Q],
               cre[0:128, jt * T + xoff:jt * T + xoff + Q],
               lmim[0:128, 0:Q], ALU.mult, r=[cre, lmim], w=[tmp])
            TT("vector", him[0:128, a0:a0 + Q], him[0:128, a0:a0 + Q],
               tmp[0:128, 0:Q], ALU.add, r=[him, tmp], w=[him])
        for md in range(8):
            first = True
            for jt in range(4):
                MM(pb[md][:, 0:Q],
                   cgre[0:128, jt * 1024 + md * 128:jt * 1024 + (md + 1) * 128],
                   hre[0:128, jt * Q:(jt + 1) * Q], start=first, stop=False,
                   r=[cgre, hre], w=[pb[md]])
                first = False
                MM(pb[md][:, 0:Q],
                   cgim[0:128, jt * 1024 + md * 128:jt * 1024 + (md + 1) * 128],
                   him[0:128, jt * Q:(jt + 1) * Q], start=False, stop=(jt == 3),
                   r=[cgim, him], w=[pb[md]])
        for md in range(8):
            STT("vector", ybf[0:128, md * Q:(md + 1) * Q],
                lnst[0:128, md * T + xoff:md * T + xoff + Q],
                dcol[:, md:md + 1], pb[md][:, 0:Q], ALU.mult, ALU.add,
                r=[lnst, dcol, pb[md]], w=[ybf])
        if c == 0:
            DUMP(hre[0:128, 0:512], hre, 512, "hre_jt0_c0")
            DUMP(ybf[0:128, 0:512], ybf, 512, "ybf_t0_c0")
            DUMP(lmre[0:128, 0:512], lmre, 512, "lmre_c0_jtlast")
            DUMP(lmim[0:128, 0:512], lmim, 512, "lmim_c0_jtlast")
        post_ln_residual(ybf, sq, tmp, xoff)

    l0.close()
    P.barrier()

    def emit_l0_ffn():
        lf = ExitStack()
        mcol = sb("mcol", (128, 16), F32, lf)
        MEMSET("vector", mcol[:, 3:4], 4.844187086458591, w=[mcol])
        scrw = {"w1q": Buf("scr_w1q"), "w2q": Buf("scr_w2q")}

        wq = ExitStack()
        wtmp = [sb("wtmp%d" % i, (128, 4096), F32, wq) for i in range(2)]
        qtmp = sb("qtmp", (128, 4096), F32, wq)
        itmp = sb("itmp", (128, 4096), I32, wq)
        rtmp = sb("rtmp", (128, 4096), F32, wq)
        dtmp = sb("dtmp", (128, 4096), BF16, wq)
        c1t = sb("c1t", (128, 4096), BF16, wq)
        qout = sb("qout", (128, 4096), BF16, wq)

        for wnm, qnm, nslab, wid, mc in (("w1t", "w1q", 8, 4096, 1),
                                         ("w2t", "w2q", 32, 1024, 2)):
            MEMSET("vector", mcol[:, 0:1], 0.0, w=[mcol])
            for k in range(nslab):
                b = wtmp[k % 2]
                LD(b[0:128, 0:wid], dr[wnm][k * 128:(k + 1) * 128, 0:wid], b)
                ACT(rtmp[0:128, 0:wid], b[0:128, 0:wid], AF.Abs, r=[b], w=[rtmp])
                RED(mcol[:, 5:6], rtmp[0:128, 0:wid], ALU.add, r=[rtmp], w=[mcol])
                TT("vector", mcol[:, 0:1], mcol[:, 0:1], mcol[:, 5:6], ALU.add,
                   r=[mcol], w=[mcol])
            MM(pb[4][0:1, 0:1], onef[:, 0:1], mcol[:, 0:1], True, True,
               r=[mcol, onef], w=[pb[4]])
            TS("vector", row(0)[:, 0:1], pb[4][0:1, 0:1], 1.0 / (1024.0 * 4096.0),
               None, ALU.mult, r=[pb[4]], w=[rowp])
            TS("vector", row(0)[:, 0:1], row(0)[:, 0:1], 1e-5, None, ALU.max,
               r=[rowp], w=[rowp])
            MM(pb[4][:, 0:1], onef[0:1, 0:128], row(0)[:, 0:1], True, True,
               r=[rowp, onef], w=[pb[4]])
            CP("vector", mcol[:, mc:mc + 1], pb[4][:, 0:1], r=[pb[4]], w=[mcol])
            ACT(mcol[:, 5:6], mcol[:, mc:mc + 1], AF.Ln, r=[mcol], w=[mcol])
            ACT(mcol[:, 0:1], mcol[:, 5:6], AF.Exp, r=[mcol], w=[mcol], scale=-1.0)
            for k in range(nslab):
                b = wtmp[k % 2]
                LD(b[0:128, 0:wid], dr[wnm][k * 128:(k + 1) * 128, 0:wid], b)
                ACT(qtmp[0:128, 0:wid], b[0:128, 0:wid], AF.Copy,
                    r=[b, mcol], w=[qtmp], scale=mcol[:, 0:1])
                CP("vector", itmp[0:128, 0:wid], qtmp[0:128, 0:wid],
                   r=[qtmp], w=[itmp])
                CP("vector", rtmp[0:128, 0:wid], itmp[0:128, 0:wid],
                   r=[itmp], w=[rtmp])
                TT("vector", dtmp[0:128, 0:wid], qtmp[0:128, 0:wid],
                   rtmp[0:128, 0:wid], ALU.subtract, r=[qtmp, rtmp], w=[dtmp])
                TS("vector", c1t[0:128, 0:wid], dtmp[0:128, 0:wid], 0.5, None,
                   ALU.is_gt, r=[dtmp], w=[c1t])
                STT("vector", dtmp[0:128, 0:wid], dtmp[0:128, 0:wid], -0.5,
                    c1t[0:128, 0:wid], ALU.is_lt, ALU.subtract,
                    r=[dtmp, c1t], w=[dtmp])
                TT("vector", rtmp[0:128, 0:wid], rtmp[0:128, 0:wid],
                   dtmp[0:128, 0:wid], ALU.subtract, r=[rtmp, dtmp], w=[rtmp])
                TS("vector", qout[0:128, 0:wid], rtmp[0:128, 0:wid], -1.0, 1.0,
                   ALU.max, r=[rtmp], w=[qout], op1=ALU.min)
                STO(dr[qnm][k * 128:(k + 1) * 128, 0:wid], qout[0:128, 0:wid],
                    qout, wbufs=[scrw[qnm]])
        P.barrier()
        wq.close()

        w1f = sb("w1f", (128, 8 * 4096), BF16, lf)
        hcur = sb("hcur", (128, 32 * 512), BF16, lf)
        hw = sb("hw", (128, 32 * 512), BF16, lf)
        hedge = sb("hedge", (128, 4 * 64), BF16, lf)
        xqt = sb("xqt", (128, 8 * 512), BF16, lf)
        lnt = sb("lnt", (128, 8 * 512), BF16, lf)
        sqf = None
        w2s = [sb("w2s%d" % i, (128, 1024), BF16, lf) for i in range(2)]
        zt = sb("zt", (128, 512), F32, lf)
        izt = sb("izt", (128, 512), I32, lf)
        rft = sb("rft", (128, 512), F32, lf)
        dqt = sb("dqt", (128, 512), BF16, lf)
        c1q = sb("c1q", (128, 512), BF16, lf)
        sqf = c1q
        tmaxf = sb("tmaxf", (128, 512), F32, lf)
        scolv = sb("scolv", (128, 16), F32, lf)
        rept = sb("rept", (128, 128), F32, lf)
        dscb = sb("dscb", (128, 512), F32, lf)
        ybf2 = lnt
        tmp2 = dqt
        m2u = sb("m2u", (128, 32), BF16, lf)
        m3u = sb("m3u", (128, 32), BF16, lf)
        mpay = sb("mpay", (128, 32), BF16, lf)
        cv1c = sb("cv1c", (128, 96), F32, lf)
        cb1c = sb("cb1c", (128, 32), F32, lf)
        LD(cv1c[:, :], dr["cv1"][:], cv1c, q="gpsimd")
        LD(cb1c[:, :], dr["cb1"][:], cb1c, q="gpsimd")
        for k in range(8):
            LD(w1f[0:128, k * 4096:(k + 1) * 4096],
               dr["w1q"][k * 128:(k + 1) * 128, 0:4096], w1f, rtls=[scrw["w1q"]])

        def tapc(t, tap):
            return cv1c[:, tap * 32 + t:tap * 32 + t + 1]

        def aq(src_fn, src_tl, ntiles, mwc, width=512):
            nblk = (width + 127) // 128
            CP("vector", tmaxf[0:128, 0:width], src_fn(0), r=[src_tl], w=[tmaxf])
            CP("vector", zt[0:128, 0:width], src_fn(0), r=[src_tl], w=[zt])
            for k in range(1, ntiles):
                TT("vector", tmaxf[0:128, 0:width], tmaxf[0:128, 0:width],
                   src_fn(k), ALU.max, r=[tmaxf, src_tl], w=[tmaxf])
                TT("vector", zt[0:128, 0:width], zt[0:128, 0:width],
                   src_fn(k), ALU.min, r=[zt, src_tl], w=[zt])
            TS("vector", zt[0:128, 0:width], zt[0:128, 0:width], -1.0, None,
               ALU.mult, r=[zt], w=[zt])
            TT("vector", tmaxf[0:128, 0:width], tmaxf[0:128, 0:width],
               zt[0:128, 0:width], ALU.max, r=[tmaxf, zt], w=[tmaxf])
            if width >= 128:
                for b in range(nblk):
                    TR(pb[4][:, b * 128:(b + 1) * 128],
                       tmaxf[0:128, b * 128:(b + 1) * 128], r=[tmaxf], w=[pb[4]])
                    RED(scolv[:, b:b + 1], pb[4][:, b * 128:(b + 1) * 128],
                        ALU.max, r=[pb[4]], w=[scolv])
                TS("vector", scolv[:, 4:4 + nblk], scolv[:, 0:nblk], 1e-5, None,
                   ALU.max, r=[scolv], w=[scolv])
                ACT(scolv[:, 8:8 + nblk], scolv[:, 4:4 + nblk], AF.Ln,
                    r=[scolv], w=[scolv])
                ACT(scolv[:, 8:8 + nblk], scolv[:, 8:8 + nblk], AF.Exp,
                    r=[scolv, mcol], w=[scolv], scale=-1.0, bias=mcol[:, 3:4])
                TS("vector", scolv[:, 12:12 + nblk], scolv[:, 4:4 + nblk],
                   1.0 / 127.0, None, ALU.mult, r=[scolv], w=[scolv])
                for b in range(nblk):
                    TT("vector", scolv[:, 12 + b:13 + b], scolv[:, 12 + b:13 + b],
                       mcol[:, mwc:mwc + 1], ALU.mult, r=[scolv, mcol], w=[scolv])
                for b in range(nblk):
                    TS("vector", rept[:, :], onef[:, 0:128], scolv[:, 8 + b:9 + b],
                       None, ALU.mult, r=[onef, scolv], w=[rept])
                    TR(pb[5][:, b * 128:(b + 1) * 128], rept[:, :], r=[rept],
                       w=[pb[5]])
                for b in range(nblk):
                    TS("vector", rept[:, :], onef[:, 0:128], scolv[:, 12 + b:13 + b],
                       None, ALU.mult, r=[onef, scolv], w=[rept])
                    TR(pb[6][:, b * 128:(b + 1) * 128], rept[:, :], r=[rept],
                       w=[pb[6]])
                CP("vector", dscb[0:128, 0:width], pb[6][:, 0:width],
                   r=[pb[6]], w=[dscb])
            else:
                TR(pb[4][0:1, 0:128], tmaxf[0:128, 0:1], r=[tmaxf], w=[pb[4]])
                RED(row(0)[:, 0:1], pb[4][0:1, 0:128], ALU.max, r=[pb[4]], w=[rowp])
                TS("vector", row(0)[:, 0:1], row(0)[:, 0:1], 1e-5, None, ALU.max,
                   r=[rowp], w=[rowp])
                ACT(row(1)[:, 0:1], row(0)[:, 0:1], AF.Ln, r=[rowp], w=[rowp])
                TS("vector", row(1)[:, 0:1], row(1)[:, 0:1], -1.0,
                   4.844187086458591, ALU.mult, r=[rowp], w=[rowp], op1=ALU.add)
                ACT(row(1)[:, 0:1], row(1)[:, 0:1], AF.Exp, r=[rowp], w=[rowp])
                MM(pb[5][:, 0:1], onef[0:1, 0:128], row(1)[:, 0:1], True, True,
                   r=[rowp, onef], w=[pb[5]])
                TS("vector", row(2)[:, 0:1], row(0)[:, 0:1], 1.0 / 127.0, None,
                   ALU.mult, r=[rowp], w=[rowp])
                MM(pb[6][:, 0:1], onef[0:1, 0:128], row(2)[:, 0:1], True, True,
                   r=[rowp, onef], w=[pb[6]])
                TS("vector", dscb[0:128, 0:1], pb[6][:, 0:1], mcol[:, mwc:mwc + 1],
                   None, ALU.mult, r=[pb[6], mcol], w=[dscb])

        def quant_tiles(src_fn, src_tl, dst_fn, dst_tl, ntiles, width=512):
            for k in range(ntiles):
                TT("vector", zt[0:128, 0:width], src_fn(k), pb[5][:, 0:width],
                   ALU.mult, r=[src_tl, pb[5]], w=[zt])
                CP("vector", izt[0:128, 0:width], zt[0:128, 0:width],
                   r=[zt], w=[izt])
                CP("vector", rft[0:128, 0:width], izt[0:128, 0:width],
                   r=[izt], w=[rft])
                TT("vector", dqt[0:128, 0:width], zt[0:128, 0:width],
                   rft[0:128, 0:width], ALU.subtract, r=[zt, rft], w=[dqt])
                TS("vector", c1q[0:128, 0:width], dqt[0:128, 0:width], 0.5, None,
                   ALU.is_gt, r=[dqt], w=[c1q])
                STT("vector", dqt[0:128, 0:width], dqt[0:128, 0:width], -0.5,
                    c1q[0:128, 0:width], ALU.is_lt, ALU.subtract,
                    r=[dqt, c1q], w=[dqt])
                TT("vector", dst_fn(k), rft[0:128, 0:width], dqt[0:128, 0:width],
                   ALU.subtract, r=[rft, dqt], w=[dst_tl])

        def emit_window(a, width, base_g, intile, intile_tl, edge_ap, edge_tl):
            l2 = a + 1 - base_g
            init_ok = (l2 >= 0) and (width <= 512 - l2)
            for t in range(32):
                if not init_ok:
                    MEMSET("vector", hw[0:128, t * 512:t * 512 + width], 0.0,
                           w=[hw])
                for tap, sft in ((2, 1), (1, 0), (0, -1)):
                    l_of = a + sft - base_g
                    j0 = max(0, -l_of)
                    jend = min(width, 512 - l_of)
                    if jend > j0:
                        o_ap = hw[0:128, t * 512 + j0:t * 512 + jend]
                        s_ap = intile(t, l_of + j0, jend - j0)
                        if tap == 2 and init_ok:
                            TS("vector", o_ap, s_ap, tapc(t, tap), None, ALU.mult,
                               r=[intile_tl, cv1c], w=[hw])
                        else:
                            STT("vector", o_ap, s_ap, tapc(t, tap), o_ap,
                                ALU.mult, ALU.add, r=[intile_tl, cv1c, hw], w=[hw])
                    for j in list(range(0, j0)) + list(range(max(jend, 0), width)):
                        STT("vector", hw[0:128, t * 512 + j:t * 512 + j + 1],
                            edge_ap(t, a + j + sft), tapc(t, tap),
                            hw[0:128, t * 512 + j:t * 512 + j + 1],
                            ALU.mult, ALU.add, r=[edge_tl, cv1c, hw], w=[hw])
                ACT(hw[0:128, t * 512:t * 512 + width],
                    hw[0:128, t * 512:t * 512 + width], AF.Silu,
                    r=[hw, cb1c], w=[hw], bias=cb1c[:, t:t + 1])
            if a == 511:
                DUMP(hw[0:128, 0:512], hw, 512, "ffn_hw_w1")
            aq(lambda k, w_=width: hw[0:128, k * 512:k * 512 + w_], hw, 32, 2, width)
            quant_tiles(lambda k, w_=width: hw[0:128, k * 512:k * 512 + w_], hw,
                        lambda k, w_=width: hw[0:128, k * 512:k * 512 + w_], hw,
                        32, width)
            for k in range(32):
                b = w2s[k % 2]
                LD(b[0:128, 0:1024], dr["w2q"][k * 128:(k + 1) * 128, 0:1024], b,
                   rtls=[scrw["w2q"]])
                for m in range(8):
                    MM(pb[m][:, 0:width], b[0:128, m * 128:(m + 1) * 128],
                       hw[0:128, k * 512:k * 512 + width],
                       start=(k == 0), stop=(k == 31), r=[b, hw], w=[pb[m]])
            for m in range(8):
                TT("vector", ybf2[0:128, m * 512:m * 512 + width],
                   pb[m][:, 0:width], dscb[0:128, 0:width], ALU.mult,
                   r=[pb[m], dscb], w=[ybf2])
            if a == 511:
                DUMP(ybf2[0:128, 0:512], ybf2, 512, "ffn_y2_w1")
            post_ln_residual2(ybf2, sqf, tmp2, a, width)

        def chunk_h(c, save_edge=True):
            xoff = c * Q
            ln_stats(lambda k, xo=xoff: x[0:128, k * T + xo:k * T + xo + Q], x, sqf)
            ln_apply(lambda k, xo=xoff: x[0:128, k * T + xo:k * T + xo + Q],
                     lambda k: lnt[0:128, k * 512:(k + 1) * 512], x, lnt)
            aq(lambda k: lnt[0:128, k * 512:(k + 1) * 512], lnt, 8, 1)
            quant_tiles(lambda k: lnt[0:128, k * 512:(k + 1) * 512], lnt,
                        lambda k: xqt[0:128, k * 512:(k + 1) * 512], xqt, 8)
            for grp in range(4):
                for m in range(8):
                    hm = grp * 8 + m
                    for k in range(8):
                        MM(pb[m][:, 0:Q],
                           w1f[0:128, k * 4096 + hm * 128:k * 4096 + (hm + 1) * 128],
                           xqt[0:128, k * 512:(k + 1) * 512],
                           start=(k == 0), stop=(k == 7), r=[w1f, xqt], w=[pb[m]])
                for m in range(8):
                    hm = grp * 8 + m
                    TT("vector", hcur[0:128, hm * 512:(hm + 1) * 512],
                       pb[m][:, 0:Q], dscb[0:128, 0:512], ALU.mult,
                       r=[pb[m], dscb], w=[hcur])
            if save_edge:
                for t in range(32):
                    CP("vector", hedge[:, c * 64 + t * 2:c * 64 + t * 2 + 2],
                       hcur[0:128, t * 512 + 510:t * 512 + 512], r=[hcur], w=[hedge])

        def send_cols(col_idx, boxa, boxb):
            for t in range(32):
                CP("vector", mpay[:, t:t + 1],
                   hcur[0:128, t * 512 + col_idx:t * 512 + col_idx + 1],
                   r=[hcur], w=[mpay])
            send_box(mpay[:, 0:32], boxa, boxb, [mpay])

        hcur_intile = lambda t, l0, n: hcur[0:128, t * 512 + l0:t * 512 + l0 + n]
        if L0F_UPTO < 2:
            P.barrier()
            lf.close()
            return
        chunk_h(0)
        send_cols(0, "m3a", "m3b")
        if L0F_UPTO < 4:
            P.barrier()
            lf.close()
            return
        chunk_h(1)
        DUMP(w1f[0:128, 0:512], w1f, 512, "ffn_w1q_s0")
        DUMP(xqt[0:128, 0:512], xqt, 512, "ffn_xq_t0c1")
        DUMP(hcur[0:128, 0:512], hcur, 512, "ffn_h_t0c1")
        DUMP(mcol[:, 0:16], mcol, 16, "ffn_mcol")
        emit_window(511, 512, 512, hcur_intile, hcur,
                    lambda t, g: hedge[:, 0 * 64 + t * 2 + (g - 510):
                                       0 * 64 + t * 2 + (g - 510) + 1], hedge)
        if L0F_UPTO < 5:
            P.barrier()
            lf.close()
            return
        chunk_h(2)
        emit_window(1023, 512, 1024, hcur_intile, hcur,
                    lambda t, g: hedge[:, 1 * 64 + t * 2 + (g - 1022):
                                       1 * 64 + t * 2 + (g - 1022) + 1], hedge)
        if L0F_UPTO < 6:
            P.barrier()
            lf.close()
            return
        chunk_h(3)
        if L0F_UPTO < 7:
            P.barrier()
            lf.close()
            return
        send_cols(511, "m2a", "m2b")
        if L0F_UPTO < 8:
            P.barrier()
            lf.close()
            return
        emit_window(1535, 512, 1536, hcur_intile, hcur,
                    lambda t, g: hedge[:, 2 * 64 + t * 2 + (g - 1534):
                                       2 * 64 + t * 2 + (g - 1534) + 1], hedge)
        if L0F_UPTO < 9:
            P.barrier()
            lf.close()
            return
        poll_box(m3u[:, 0:32], "m3a", "m3b", [m3u])
        TS("vector", m3u[:, 0:32], m3u[:, 0:32], gcol[:, 1:2], None, ALU.mult,
           r=[m3u, gcol], w=[m3u])
        if L0F_UPTO < 95:
            P.barrier()
            lf.close()
            return
        emit_window(2047, 1, 1536, hcur_intile, hcur,
                    lambda t, g: m3u[:, t:t + 1], m3u)
        if L0F_UPTO < 10:
            P.barrier()
            lf.close()
            return
        chunk_h(0, save_edge=False)
        for t in range(32):
            CP("vector", hcur[0:128, t * 512 + 511:t * 512 + 512],
               hedge[:, t * 2 + 1:t * 2 + 2], r=[hedge], w=[hcur])
        poll_box(m2u[:, 0:32], "m2a", "m2b", [m2u])
        TS("vector", m2u[:, 0:32], m2u[:, 0:32], gcol[:, 0:1], None, ALU.mult,
           r=[m2u, gcol], w=[m2u])
        emit_window(0, 511, 0, hcur_intile, hcur,
                    lambda t, g: m2u[:, t:t + 1], m2u)
        P.barrier()
        lf.close()

    if BUILD_UPTO >= 2:
        emit_l0_ffn()
        P.barrier()

    def emit_l1_attn():
        l1 = ExitStack()
        xcown = sb("xcown", (128, 8 * T), BF16, l1)
        wxpb = sb("wxpb", (128, 8 * 33), BF16, l1)
        woutb = sb("woutb", (128, 8 * 1024), BF16, l1)
        m4u = sb("m4u", (128, 24), BF16, l1)
        acol = sb("acol", (128, 64), F32, l1)
        cv4c = sb("cv4c", (128, 32), F32, l1)
        cb4c = sb("cb4c", (128, 8), F32, l1)
        dpc = sb("dpc", (128, 8), F32, l1)
        idh = sb("idh", (128, 128), BF16, l1)
        scrN = {"winb": Buf("scr_winb"), "zscr": Buf("scr_zscr")}

        LD(cv4c[:, :], dr["cv4"][:], cv4c, q="gpsimd")
        LD(cb4c[:, :], dr["cb4"][:], cb4c, q="gpsimd")
        LD(dpc[:, :], dr["dpar"][:], dpc, q="gpsimd")
        LD(acol[:, :], dr["loga"][:], acol, q="gpsimd")
        ACT(acol[:, :], acol[:, :], AF.Exp, r=[acol], w=[acol])
        TS("vector", acol[:, :], acol[:, :], -1.0, None, ALU.mult,
           r=[acol], w=[acol])
        CP("vector", idh[:, :], idf[:, :], r=[idf], w=[idh])

        la = ExitStack()
        lnt1 = sb("lnt1", (128, 8 * Q), BF16, la)
        wstg = sb("wstg", (128, 2048), F32, la)
        wsb = sb("wsb", (128, 2048), BF16, la)
        wsl = [sb("wsl%d" % i, (128, 1024), BF16, la) for i in range(2)]
        xmc = sb("xmc", (128, 8 * Q), BF16, la)
        cacc = sb("cacc", (128, Q), F32, la)
        xedge4 = sb("xedge4", (128, 24), BF16, la)
        xm0 = sb("xm0", (128, 24), BF16, la)
        zt2 = [sb("zt2_%d" % i, (128, Q), BF16, la) for i in range(2)]
        sqh = sb("sqh", (128, Q), BF16, la)

        for k in range(8):
            LD(wstg[:, :], dr["win"][k * 128:(k + 1) * 128, :], wstg)
            CP("scalar", wsb[:, :], wstg[:, :], r=[wstg], w=[wsb])
            STO(dr["winb"][k * 128:(k + 1) * 128, :], wsb[:, :], wsb,
                wbufs=[scrN["winb"]])
        for k in range(8):
            LD(wstg[:, 0:33], dr["wxp"][k * 128:(k + 1) * 128, :], wstg)
            CP("scalar", wxpb[0:128, k * 33:(k + 1) * 33], wstg[:, 0:33],
               r=[wstg], w=[wxpb])
        for k in range(8):
            LD(wstg[:, 0:1024], dr["wout"][k * 128:(k + 1) * 128, :], wstg)
            CP("scalar", woutb[0:128, k * 1024:(k + 1) * 1024], wstg[:, 0:1024],
               r=[wstg], w=[woutb])

        for c in range(NCH):
            xoff = c * Q
            ln_stats(lambda k, xo=xoff: x[0:128, k * T + xo:k * T + xo + Q], x, sqh)
            ln_apply(lambda k, xo=xoff: x[0:128, k * T + xo:k * T + xo + Q],
                     lambda k: lnt1[0:128, k * Q:(k + 1) * Q], x, lnt1)
            for grp in range(2):
                for k in range(8):
                    b = wsl[k % 2]
                    LD(b[:, :], dr["winb"][k * 128:(k + 1) * 128,
                                           grp * 1024:(grp + 1) * 1024], b,
                       rtls=[scrN["winb"]])
                    for m in range(8):
                        MM(pb[m][:, 0:Q], b[0:128, m * 128:(m + 1) * 128],
                           lnt1[0:128, k * Q:(k + 1) * Q],
                           start=(k == 0), stop=(k == 7), r=[b, lnt1], w=[pb[m]])
                if grp == 0:
                    for m in range(8):
                        CP("vector", xmc[0:128, m * Q:(m + 1) * Q], pb[m][:, 0:Q],
                           r=[pb[m]], w=[xmc])
                else:
                    for m in range(8):
                        zb = zt2[m % 2]
                        CP("vector", zb[0:128, 0:Q], pb[m][:, 0:Q],
                           r=[pb[m]], w=[zb])
                        STO(dr["zscr"][m * 128:(m + 1) * 128, xoff:xoff + Q],
                            zb[0:128, 0:Q], zb, wbufs=[scrN["zscr"]])
            if c == 0:
                for dt in range(8):
                    CP("vector", xm0[:, dt * 3:dt * 3 + 3],
                       xmc[0:128, dt * Q:dt * Q + 3], r=[xmc], w=[xm0])
            for dt in range(8):
                TS("vector", cacc[0:128, 0:Q], xmc[0:128, dt * Q:(dt + 1) * Q],
                   cv4c[:, 24 + dt:25 + dt], None, ALU.mult,
                   r=[xmc, cv4c], w=[cacc])
                for kk, sft in ((2, 1), (1, 2), (0, 3)):
                    STT("vector", cacc[0:128, sft:Q],
                        xmc[0:128, dt * Q:dt * Q + Q - sft],
                        cv4c[:, kk * 8 + dt:kk * 8 + dt + 1],
                        cacc[0:128, sft:Q], ALU.mult, ALU.add,
                        r=[xmc, cv4c, cacc], w=[cacc])
                    if c > 0:
                        for j in range(sft):
                            STT("vector", cacc[0:128, j:j + 1],
                                xedge4[:, dt * 3 + 3 - sft + j:
                                       dt * 3 + 4 - sft + j],
                                cv4c[:, kk * 8 + dt:kk * 8 + dt + 1],
                                cacc[0:128, j:j + 1], ALU.mult, ALU.add,
                                r=[xedge4, cv4c, cacc], w=[cacc])
                ACT(xcown[0:128, dt * T + xoff:dt * T + xoff + Q],
                    cacc[0:128, 0:Q], AF.Silu, r=[cacc, cb4c], w=[xcown],
                    bias=cb4c[:, dt:dt + 1])
            for dt in range(8):
                CP("vector", xedge4[:, dt * 3:dt * 3 + 3],
                   xmc[0:128, dt * Q + Q - 3:dt * Q + Q], r=[xmc], w=[xedge4])

        DUMP(xedge4[:, 0:24], xedge4, 24, "l1_xedge")
        send_box(xedge4[:, 0:24], "m4a", "m4b", [xedge4])
        poll_box(m4u[:, 0:24], "m4a", "m4b", [m4u])
        DUMP(m4u[:, 0:24], m4u, 24, "l1_m4u_pre")
        TS("vector", m4u[:, 0:24], m4u[:, 0:24], gcol[:, 0:1], None, ALU.mult,
           r=[m4u, gcol], w=[m4u])
        DUMP(m4u[:, 0:24], m4u, 24, "l1_m4u")
        DUMP(xm0[:, 0:24], xm0, 24, "l1_xm0")
        for dt in range(8):
            for j in range(3):
                first = True
                for kk in range(4):
                    si = j - 3 + kk
                    src = (m4u[:, dt * 3 + 3 + si:dt * 3 + 4 + si] if si < 0
                           else xm0[:, dt * 3 + si:dt * 3 + si + 1])
                    stl = m4u if si < 0 else xm0
                    if first:
                        TS("vector", cacc[0:128, j:j + 1], src,
                           cv4c[:, kk * 8 + dt:kk * 8 + dt + 1], None, ALU.mult,
                           r=[stl, cv4c], w=[cacc])
                        first = False
                    else:
                        STT("vector", cacc[0:128, j:j + 1], src,
                            cv4c[:, kk * 8 + dt:kk * 8 + dt + 1],
                            cacc[0:128, j:j + 1], ALU.mult, ALU.add,
                            r=[stl, cv4c, cacc], w=[cacc])
            ACT(xcown[0:128, dt * T:dt * T + 3], cacc[0:128, 0:3], AF.Silu,
                r=[cacc, cb4c], w=[xcown], bias=cb4c[:, dt:dt + 1])
        DUMP(xcown[0:128, 0:512], xcown, 512, "l1_xcown")
        P.barrier()
        la.close()

        lb = ExitStack()
        pA = sb("pA", (32, T), BF16, lb)
        dhh = sb("dhh", (1, T), BF16, lb)
        for c in range(NCH):
            xoff = c * Q
            for k in range(8):
                MM(pb[0][0:32, 0:Q], wxpb[0:128, k * 33:k * 33 + 32],
                   xcown[0:128, k * T + xoff:k * T + xoff + Q],
                   start=(k == 0), stop=(k == 7), r=[wxpb, xcown], w=[pb[0]])
            for k in range(8):
                MM(pb[1][0:1, 0:Q], wxpb[0:128, k * 33 + 32:k * 33 + 33],
                   xcown[0:128, k * T + xoff:k * T + xoff + Q],
                   start=(k == 0), stop=(k == 7), r=[wxpb, xcown], w=[pb[1]])
            CP("vector", pA[0:32, xoff:xoff + Q], pb[0][0:32, 0:Q],
               r=[pb[0]], w=[pA])
            CP("vector", rowp[0:1, xoff:xoff + Q], pb[1][0:1, 0:Q],
               r=[pb[1]], w=[rowp])
        ACT(rowp[0:1, 0:T], rowp[0:1, 0:T], AF.Exp, r=[rowp], w=[rowp])
        ACT(rowp[0:1, 0:T], rowp[0:1, 0:T], AF.Ln, r=[rowp, onef], w=[rowp],
            bias=onef[0:1, 0:1])
        CP("vector", dhh[0:1, 0:T], rowp[0:1, 0:T], r=[rowp], w=[dhh])
        DUMP(pA[0:32, 0:512], pA, 512, "l1_pa", npart=32)
        send_multi([(lambda bx: dr[bx][0:32, :], pA[0:32, 0:T]),
                    (lambda bx: dr[bx][32:33, :], dhh[0:1, 0:T])],
                   "m5ra", "m5rb", [pA, dhh])
        send_multi([(lambda bx: dr[bx].rearrange("(a p) t -> p a t", p=128),
                     xcown[0:128, 0:8 * T].rearrange("p (a t) -> p a t", a=8))],
                   "m5a", "m5b", [xcown])
        P.barrier()
        lb.close()

        ls = ExitStack()
        xcg = [sb("xcg%d" % i, (128, 8 * Q), BF16, ls) for i in range(2)]
        bcT = sb("bcT", (128, 68), BF16, ls)
        bcTf = sb("bcTf", (128, 68), F32, ls)
        rep2 = sb("rep2", (128, 128), F32, ls)
        dbcs = sb("dbcs", (128, Q), F32, ls)
        bbc = sb("bbc", (128, Q), F32, ls)
        cbc = sb("cbc", (128, Q), F32, ls)
        af = sb("af", (128, Q), BF16, ls)
        buf_ = sb("buf_", (128, Q), BF16, ls)
        hso = sb("hso", (128, Q), BF16, ls)
        car2 = sb("car2", (128, 64), F32, ls)
        ysb = sb("ysb", (128, 2 * Q), F32, ls)
        ytmp = sb("ytmp", (128, 8 * Q), BF16, ls)
        yown = sb("yown", (128, 8 * T), BF16, ls)
        ytin = sb("ytin", (128, 8 * Q), BF16, ls)
        zld = sb("zld", (128, 8 * Q), BF16, ls)
        zsl = sb("zsl", (128, Q), BF16, ls)
        ywk = sb("ywk", (128, Q), F32, ls)
        yfin = sb("yfin", (128, 8 * Q), BF16, ls)
        sqs = sb("sqs", (128, Q), BF16, ls)
        tmps = sb("tmps", (128, Q), BF16, ls)

        MEMSET("vector", car2[:, :], 0.0, w=[car2])
        for fl in ("m5a_f", "m5b_f", "m5ra_f", "m5rb_f"):
            poll_flag(fl)

        def box5(g):
            return ("m5a", g * Q) if g < 4 else ("m5b", (g - 4) * Q)

        def ld_xcg(g):
            bx, off = box5(g)
            LD(xcg[g % 2][0:128, 0:8 * Q].rearrange("p (a t) -> p a t", a=8),
               dr[bx][:, off:off + Q].rearrange("(a p) t -> p a t", p=128),
               xcg[g % 2], rtls=[mbB[bx], frd])

        ld_xcg(0)
        for g in range(8):
            if g < 7:
                ld_xcg(g + 1)
            bxr, offr = ("m5ra", g * Q) if g < 4 else ("m5rb", (g - 4) * Q)
            def bc_pairs(r0):
                ps = []
                for b in range(4):
                    o0 = offr + b * 128
                    ps.append((bcT[0:128, b * 8:b * 8 + 8],
                               dr[bxr][8 * r0:8 * r0 + 8, o0:o0 + 128]
                               .rearrange("n t -> t n")))
                    ps.append((bcT[0:128, 32 + b * 8:32 + b * 8 + 8],
                               dr[bxr][16 + 8 * r0:16 + 8 * r0 + 8, o0:o0 + 128]
                               .rearrange("n t -> t n")))
                return ps
            branched_dma(bc_pairs(0), bc_pairs(1),
                         rbufs=[mbB[bxr], frd.b], wbufs=[bcT.b], slow=True)
            for b in range(4):
                LD(bcT[0:128, 64 + b:65 + b],
                   dr[bxr][32:33, offr + b * 128:offr + (b + 1) * 128]
                   .rearrange("o t -> t o"),
                   bcT, rtls=[mbB[bxr], frd], slow=True)
            CP("vector", bcTf[:, :], bcT[:, :], r=[bcT], w=[bcTf])
            if g == 0:
                DUMP(bcTf[0:128, 0:68], bcTf, 68, "l1_bcTf")
            for b in range(4):
                TS("vector", rep2[:, :], onef[:, 0:128], bcTf[:, 64 + b:65 + b],
                   None, ALU.mult, r=[onef, bcTf], w=[rep2])
                TR(pb[6][:, b * 128:(b + 1) * 128], rep2[:, :], r=[rep2], w=[pb[6]])
            CP("vector", dbcs[0:128, 0:Q], pb[6][:, 0:Q], r=[pb[6]], w=[dbcs])
            if g == 0:
                DUMP(dbcs[0:128, 0:512], dbcs, 512, "l1_dbcs")
            for nl in range(8):
                for b in range(4):
                    TS("vector", rep2[:, :], onef[:, 0:128],
                       bcTf[:, b * 8 + nl:b * 8 + nl + 1], None, ALU.mult,
                       r=[onef, bcTf], w=[rep2])
                    TR(pb[6][:, b * 128:(b + 1) * 128], rep2[:, :],
                       r=[rep2], w=[pb[6]])
                TT("vector", bbc[0:128, 0:Q], pb[6][:, 0:Q], dbcs[0:128, 0:Q],
                   ALU.mult, r=[pb[6], dbcs], w=[bbc])
                for b in range(4):
                    TS("vector", rep2[:, :], onef[:, 0:128],
                       bcTf[:, 32 + b * 8 + nl:33 + b * 8 + nl], None, ALU.mult,
                       r=[onef, bcTf], w=[rep2])
                    TR(pb[7][:, b * 128:(b + 1) * 128], rep2[:, :],
                       r=[rep2], w=[pb[7]])
                CP("vector", cbc[0:128, 0:Q], pb[7][:, 0:Q], r=[pb[7]], w=[cbc])
                for dt in range(8):
                    nc_ = dt * 8 + nl
                    ACT(af[0:128, 0:Q], dbcs[0:128, 0:Q], AF.Exp,
                        r=[dbcs, acol], w=[af], scale=acol[:, nc_:nc_ + 1])
                    TT("vector", buf_[0:128, 0:Q],
                       xcg[g % 2][0:128, dt * Q:(dt + 1) * Q], bbc[0:128, 0:Q],
                       ALU.mult, r=[xcg[g % 2], bbc], w=[buf_])
                    SCAN(hso[0:128, 0:Q], af[0:128, 0:Q], buf_[0:128, 0:Q],
                         car2[:, nc_:nc_ + 1], r=[af, buf_, car2], w=[hso])
                    CP("vector", car2[:, nc_:nc_ + 1], hso[0:128, Q - 1:Q],
                       r=[hso], w=[car2])
                    TT("vector", hso[0:128, 0:Q], hso[0:128, 0:Q],
                       cbc[0:128, 0:Q], ALU.mult, r=[hso, cbc], w=[hso])
                    if dt < 6:
                        MM(pb[dt][:, 0:Q], idh[:, :], hso[0:128, 0:Q],
                           start=(nl == 0), stop=(nl == 7),
                           r=[idh, hso], w=[pb[dt]])
                    elif nl == 0:
                        CP("vector", ysb[0:128, (dt - 6) * Q:(dt - 5) * Q],
                           hso[0:128, 0:Q], r=[hso], w=[ysb])
                    else:
                        TT("vector", ysb[0:128, (dt - 6) * Q:(dt - 5) * Q],
                           ysb[0:128, (dt - 6) * Q:(dt - 5) * Q],
                           hso[0:128, 0:Q], ALU.add, r=[ysb, hso], w=[ysb])
            for dt in range(6):
                CP("vector", ytmp[0:128, dt * Q:(dt + 1) * Q], pb[dt][:, 0:Q],
                   r=[pb[dt]], w=[ytmp])
            for dt in (6, 7):
                CP("vector", ytmp[0:128, dt * Q:(dt + 1) * Q],
                   ysb[0:128, (dt - 6) * Q:(dt - 5) * Q], r=[ysb], w=[ytmp])
            if g == 0:
                DUMP(ytmp[0:128, 0:512], ytmp, 512, "l1_ytmp0")
            loc = (g % 4) * 4096
            branched_dma(
                [((yown[0:128, loc:loc + 4096] if g < 4
                   else dr["m6b"][0:128, loc:loc + 4096]), ytmp[0:128, 0:4096])],
                [((dr["m6a"][0:128, loc:loc + 4096] if g < 4
                   else yown[0:128, loc:loc + 4096]), ytmp[0:128, 0:4096])],
                rbufs=[ytmp.b], wbufs=[yown.b, mbB["m6a"], mbB["m6b"]])

        def fyflag(g):
            r_role = get_regs(g)[0]
            g.reg_load(r_role, rolesb[0:1, 0:1])
            with g.If_eq(r_role, 0):
                g.dma_start(out=dr["m6b_f"][:], in_=flag1[:]).then_inc(mbsem, 16)
            with g.Else():
                g.dma_start(out=dr["m6a_f"][:], in_=flag1[:]).then_inc(mbsem, 16)
            g.end_ifs()
            mbst["sends"] += 1
            g.wait_ge(mbsem, mbst["sends"] * 16)
            return g.nop()
        P.op("gpsimd", fyflag, reads=[rolesb.b, flag1.b, mbB["m6a"], mbB["m6b"]],
             writes=[mbB["m6a_f"], mbB["m6b_f"]])

        def fpoll6(g):
            r_role, r_cnt, r_f, r_p = get_regs(g)
            g.reg_load(r_role, rolesb[0:1, 0:1])
            def cond():
                g.reg_add(r_cnt, r_cnt, 16)
                with g.If_eq(r_role, 0):
                    g.dma_start(out=frd[:], in_=dr["m6a_f"][:]).then_inc(psem, 16)
                with g.Else():
                    g.dma_start(out=frd[:], in_=dr["m6b_f"][:]).then_inc(psem, 16)
                g.end_ifs()
                g.wait_ge(psem, r_cnt)
                g.reg_load(r_f, frd[0:1, 0:1])
                g.reg_alu(r_p, r_f, magic, ALU.not_equal)
                return r_p
            with g.While(cond):
                g.nop()
            return g.nop()
        P.op("gpsimd", fpoll6, reads=[rolesb.b, frd.b, mbB["m6a_f"],
                                      mbB["m6b_f"]], writes=[frd.b])

        for c in range(NCH):
            xoff = c * Q
            branched_dma(
                [(ytin[0:128, 0:8 * Q], dr["m6a"][0:128, c * 4096:(c + 1) * 4096])],
                [(ytin[0:128, 0:8 * Q], dr["m6b"][0:128, c * 4096:(c + 1) * 4096])],
                rbufs=[mbB["m6a"], mbB["m6b"], frd.b], wbufs=[ytin.b])
            LD(zld[0:128, 0:8 * Q].rearrange("p (a t) -> p a t", a=8),
               dr["zscr"][:, xoff:xoff + Q].rearrange("(a p) t -> p a t", p=128),
               zld, rtls=[scrN["zscr"]])
            for dt in range(8):
                TT("vector", ywk[0:128, 0:Q],
                   yown[0:128, c * 4096 + dt * Q:c * 4096 + (dt + 1) * Q],
                   ytin[0:128, dt * Q:(dt + 1) * Q], ALU.add,
                   r=[yown, ytin], w=[ywk])
                STT("vector", ywk[0:128, 0:Q],
                    xcown[0:128, dt * T + xoff:dt * T + xoff + Q],
                    dpc[:, dt:dt + 1], ywk[0:128, 0:Q], ALU.mult, ALU.add,
                    r=[xcown, dpc, ywk], w=[ywk])
                ACT(zsl[0:128, 0:Q], zld[0:128, dt * Q:(dt + 1) * Q], AF.Silu,
                    r=[zld], w=[zsl])
                TT("vector", yfin[0:128, dt * Q:(dt + 1) * Q], ywk[0:128, 0:Q],
                   zsl[0:128, 0:Q], ALU.mult, r=[ywk, zsl], w=[yfin])
            if c == 0:
                DUMP(yfin[0:128, 0:512], yfin, 512, "l1_yfin")
            for m in range(8):
                for k in range(8):
                    MM(pb[m][:, 0:Q],
                       woutb[0:128, k * 1024 + m * 128:k * 1024 + (m + 1) * 128],
                       yfin[0:128, k * Q:(k + 1) * Q],
                       start=(k == 0), stop=(k == 7), r=[woutb, yfin], w=[pb[m]])
            for m in range(8):
                CP("vector", ytmp[0:128, m * Q:(m + 1) * Q], pb[m][:, 0:Q],
                   r=[pb[m]], w=[ytmp])
            post_ln_residual2(ytmp, sqs, tmps, xoff, Q)
        P.barrier()
        ls.close()
        l1.close()

    if BUILD_UPTO >= 3:
        emit_l1_attn()
        P.barrier()

    def emit_l1_ffn():
        lf = ExitStack()
        scrF = {"wgb": Buf("scr_wgb"), "wub": Buf("scr_wub"),
                "wdb": Buf("scr_wdb")}
        ca = ExitStack()
        wsti = sb("wsti", (128, H2), I32, ca)
        wstf = sb("wstf", (128, H2), F32, ca)
        wsth = sb("wsth", (128, H2), BF16, ca)
        for src, dst in (("wgt", "wgb"), ("wut", "wub")):
            for k in range(8):
                LD(wsti[:, :], dr[src][k * 128:(k + 1) * 128, :], wsti)
                CP("vector", wstf[:, :], wsti[:, :], r=[wsti], w=[wstf])
                CP("scalar", wsth[:, :], wstf[:, :], r=[wstf], w=[wsth])
                STO(dr[dst][k * 128:(k + 1) * 128, :], wsth[:, :], wsth,
                    wbufs=[scrF[dst]])
        for k in range(24):
            LD(wsti[:, 0:D], dr["wdt"][k * 128:(k + 1) * 128, :], wsti)
            CP("vector", wstf[:, 0:D], wsti[:, 0:D], r=[wsti], w=[wstf])
            CP("scalar", wsth[:, 0:D], wstf[:, 0:D], r=[wstf], w=[wsth])
            STO(dr["wdb"][k * 128:(k + 1) * 128, :], wsth[:, 0:D], wsth,
                wbufs=[scrF["wdb"]])
        P.barrier()
        ca.close()

        lnt2 = sb("lnt2", (128, 8 * Q), BF16, lf)
        hblk = sb("hblk", (128, 24 * Q), BF16, lf)
        wgs = [sb("wgs%d" % i, (128, 1024), BF16, lf) for i in range(2)]
        wds = [sb("wds%d" % i, (128, 1024), BF16, lf) for i in range(2)]
        ybf3 = sb("ybf3", (128, 8 * Q), BF16, lf)
        sqf2 = sb("sqf2", (128, Q), BF16, lf)
        tmpf2 = sb("tmpf2", (128, Q), BF16, lf)
        SC = 1.0 / 127.0

        for c in range(NCH):
            xoff = c * Q
            ln_stats(lambda k, xo=xoff: x[0:128, k * T + xo:k * T + xo + Q],
                     x, sqf2)
            ln_apply(lambda k, xo=xoff: x[0:128, k * T + xo:k * T + xo + Q],
                     lambda k: lnt2[0:128, k * Q:(k + 1) * Q], x, lnt2)
            for grp in range(3):
                for k in range(8):
                    b = wgs[k % 2]
                    LD(b[:, :], dr["wgb"][k * 128:(k + 1) * 128,
                                          grp * 1024:(grp + 1) * 1024], b,
                       rtls=[scrF["wgb"]])
                    for m in range(8):
                        MM(pb[m][:, 0:Q], b[0:128, m * 128:(m + 1) * 128],
                           lnt2[0:128, k * Q:(k + 1) * Q],
                           start=(k == 0), stop=(k == 7), r=[b, lnt2], w=[pb[m]])
                for m in range(8):
                    hm = grp * 8 + m
                    ACT(hblk[0:128, hm * Q:(hm + 1) * Q], pb[m][:, 0:Q],
                        AF.Silu, r=[pb[m]], w=[hblk], scale=SC)
                for k in range(8):
                    b = wgs[k % 2]
                    LD(b[:, :], dr["wub"][k * 128:(k + 1) * 128,
                                          grp * 1024:(grp + 1) * 1024], b,
                       rtls=[scrF["wub"]])
                    for m in range(8):
                        MM(pb[m][:, 0:Q], b[0:128, m * 128:(m + 1) * 128],
                           lnt2[0:128, k * Q:(k + 1) * Q],
                           start=(k == 0), stop=(k == 7), r=[b, lnt2], w=[pb[m]])
                for m in range(8):
                    hm = grp * 8 + m
                    STT("vector", hblk[0:128, hm * Q:(hm + 1) * Q],
                        pb[m][:, 0:Q], SC, hblk[0:128, hm * Q:(hm + 1) * Q],
                        ALU.mult, ALU.mult, r=[pb[m], hblk], w=[hblk])
            for k in range(24):
                b = wds[k % 2]
                LD(b[:, :], dr["wdb"][k * 128:(k + 1) * 128, :], b,
                   rtls=[scrF["wdb"]])
                for m in range(8):
                    MM(pb[m][:, 0:Q], b[0:128, m * 128:(m + 1) * 128],
                       hblk[0:128, k * Q:(k + 1) * Q],
                       start=(k == 0), stop=(k == 23), r=[b, hblk], w=[pb[m]])
            for m in range(8):
                TS("vector", ybf3[0:128, m * Q:(m + 1) * Q], pb[m][:, 0:Q],
                   SC, None, ALU.mult, r=[pb[m]], w=[ybf3])
            post_ln_residual2(ybf3, sqf2, tmpf2, xoff, Q)
        P.barrier()
        lf.close()

    if BUILD_UPTO >= 4:
        emit_l1_ffn()
        P.barrier()

    so = STO(xout.rearrange("(a p) t -> p a t", p=128), x[:, 0:8 * T].rearrange("p (a t) -> p a t", a=8), x)
    P.finalize()
    P.emit(nc, ctx)
    ctx.close()
    return nc


def kernel(**inputs):
    import time
    nonce_len = (int(time.time() * 1000) % 997) + 1
    magic = (int(time.time() * 7919) % 999983) + 7
    maps = stage_inputs(inputs)
    for m in maps:
        m["nonce"] = np.zeros((1, nonce_len), np.float32)
    nc = build_program(nonce_len, magic)
    res = run_bass_kernel_spmd(nc, maps, core_ids=list(range(8)))
    u = np.asarray(inputs["L0_attn_u"], np.float32)
    out = np.empty((4, 4096, 1024), np.float32)
    for core in range(8):
        b, h = core // 2, core % 2
        out[b, h * T:(h + 1) * T, :] = res.results[core]["xout"].T
    return out


if __name__ == "__main__":
    print("build test")
    nc = build_program()
    print("built ops")
